# version 02
# baseline (speedup 1.0000x reference)
import math
import numpy as np

import concourse.bass as bass
import concourse.mybir as mybir
from concourse.bass_utils import run_bass_kernel_spmd

F32 = mybir.dt.float32
BF16 = mybir.dt.bfloat16
Alu = mybir.AluOpType
AF = mybir.ActivationFunctionType
Ax = mybir.AxisListType

D = 1024
H = 16
HD = 64
NF = 8
NPAIR = 8
NSEQ = 2

def _hl(v):
    hi = np.float32(v)
    lo = np.float32(v - np.float64(hi))
    return float(hi), float(lo)

C127_HI, C127_LO = _hl(1.0 / 127.0)
INV2PI = float(np.float32(1.0 / (2.0 * math.pi)))
def _rnd_bits(v, bits):
    m, e = math.frexp(v)
    scale = 2.0 ** (bits - e)
    return math.floor(v * scale + 0.5) / scale
TWO_PI = 2.0 * math.pi
CW1 = float(np.float32(_rnd_bits(TWO_PI, 14)))
CW2 = float(np.float32(_rnd_bits(TWO_PI - CW1, 14)))
CW3 = float(np.float32(TWO_PI - CW1 - CW2))
PIO2_HI, PIO2_LO = _hl(math.pi / 2.0)
PI_CLAMP = float(np.nextafter(np.float32(math.pi), np.float32(0)))
ROPE_F = [10000.0 ** (-i / 32.0) for i in range(32)]
ROPE_F_HL = [_hl(f) for f in ROPE_F]
ELU_SCALE = 0.125
C32MAGIC = float(np.float32(1.5 * 2.0 ** 23))
C53MAGIC = float(np.float32(1.5 * 2.0 ** 52))
T_EPS = float(np.float32(2.0 ** -30))
T_SCALE = float(np.float32(2.0 ** 30))


def emit_round(w, y_ap, out_ap, m_ap, u_ap):
    b = w.b
    w.v('tensor_scalar', m_ap, y_ap, C32MAGIC, -C32MAGIC, Alu.add, Alu.add)
    w.v('tensor_scalar', u_ap, y_ap, C53MAGIC, -C53MAGIC, Alu.add, Alu.add)
    w.drain('vector')
    w.v('tensor_tensor', u_ap, u_ap, m_ap, Alu.subtract)
    w.drain('vector')
    ins = w.v('scalar_tensor_tensor', out_ap, u_ap, b['stat'][:, 12:13], m_ap,
              Alu.mult, Alu.add)
    w.drain('vector')
    return ins


class W:

    def __init__(self, nc, eng_name, eng, sems, bufs, seq_t):
        self.nc = nc
        self.en = eng_name
        self.eng = eng
        self.sems = sems
        self.b = bufs
        self.seq_t = seq_t
        self.ntiles = seq_t // 128
        self.c = {k: 0 for k in sems}
        self.rg_idx = 0
        self.bfc = [0, 0, 0, 0]
        self.tok = {}

    def on(self, name):
        return self.en == name

    def _e(self, name, meth, *a, **k):
        if self.en == name:
            return getattr(self.eng, meth)(*a, **k)
        return None

    def t(self, meth, *a, **k):
        return self._e('tensor', meth, *a, **k)

    def v(self, meth, *a, **k):
        return self._e('vector', meth, *a, **k)

    def s(self, meth, *a, **k):
        return self._e('scalar', meth, *a, **k)

    def g(self, meth, *a, **k):
        return self._e('gpsimd', meth, *a, **k)

    def drain(self, name):
        if self.en == name:
            self.eng.drain()

    def wait(self, name, sem, val):
        if val > 0 and self.en == name:
            self.eng.wait_ge(self.sems[sem], val)

    def inc(self, name, sem, n, instr):
        self.c[sem] += n
        if self.en == name and instr is not None:
            instr.then_inc(self.sems[sem], n)
            self.eng.wait_ge(self.sems[sem], self.c[sem])
        return self.c[sem]

    def dma(self, qeng, sem, out_ap, in_ap):
        self.c[sem] += 16
        if self.en == qeng:
            self.eng.dma_start(out=out_ap, in_=in_ap).then_inc(self.sems[sem], 16)
        return self.c[sem]

    def rotg(self, mms_fn, evac_eng, evac_fn):
        bi = self.rg_idx % 4
        self.rg_idx += 1
        bank = self.b['pw'][bi]
        self.wait('tensor', 'B%d' % bi, self.bfc[bi])
        last = mms_fn(bank)
        tok = self.inc('tensor', 'TM', 1, last)
        self.wait(evac_eng, 'TM', tok)
        ev = evac_fn(bank)
        self.bfc[bi] += 1
        self.inc(evac_eng, 'B%d' % bi, 1, ev)
        sem = {'scalar': 'S', 'vector': 'V'}[evac_eng]
        d = self._e(evac_eng, 'drain')
        self.inc(evac_eng, sem, 1, d)
        return tok


def _ap(t, off, dims):
    return bass.AP(t, off, dims)


def build_nc(seq_t=4096):
    nc = bass.Bass("TRN2", target_bir_lowering=False)
    T = seq_t
    x_d = nc.dram_tensor("x", [NSEQ, T, D], F32, kind="ExternalInput")
    wqkv_d = nc.dram_tensor("wqkv", [3 * D, D], F32, kind="ExternalInput")
    wout_d = nc.dram_tensor("wout", [D, D], F32, kind="ExternalInput")
    frhs_d = nc.dram_tensor("frhs", [D, D], F32, kind="ExternalInput")
    y_d = nc.dram_tensor("y", [NSEQ, T, D], F32, kind="ExternalOutput")
    qf_d = nc.dram_tensor("qfs", [NSEQ, T, D], F32, kind="Internal")
    pt_d = nc.dram_tensor("part", [NSEQ, T, D], F32, kind="Internal")

    al = nc.alloc_sbuf_tensor
    bufs = dict(
        arena=al("arena", [128, 24576], F32),
        qrhs=al("qrhs", [128, 8192], BF16),
        rsb=al("rsb", [128, D], F32),
        ctab=al("ctab", [128, D], F32),
        stab=al("stab", [128, 2048], F32),
        identF=al("identF", [128, 128], F32),
        identB=al("identB", [128, 128], BF16),
        onesF=al("onesF", [128, 1], F32),
        onesR=al("onesR", [1, 128], F32),
        ksrow=al("ksrow", [1, 2 * D], F32),
        ksb=al("ksb", [128, D], F32),
        vksb=al("vksb", [128, D], F32),
        xa=al("xa", [128, D], F32),
        xb=al("xb", [128, D], F32),
        m1=al("m1", [128, D], F32),
        m2=al("m2", [128, D], F32),
        xn=al("xn", [128, D], F32),
        qt=al("qt", [128, D], F32),
        kr=al("kr", [128, D], F32),
        qr=al("qr", [128, D], F32),
        sA=al("sA", [128, D], F32),
        sB=al("sB", [128, D], F32),
        sG=al("sG", [128, D], F32),
        xfm=al("xfm", [128, D], F32),
        qq=al("qq", [128, D], BF16),
        qqf=al("qqf", [128, D], BF16),
        stat=al("stat", [128, 64], F32),
        tcol=al("tcol", [128, 32], F32),
        cEPS=al("cEPS", [128, 1], F32),
        cQS=al("cQS", [128, 1], F32),
    )
    ap = nc.alloc_psum_tensor
    bufs['pw'] = [ap("pw%d" % i, [128, 512], F32) for i in range(4)]
    bufs['pkv'] = [ap("pkv%d" % i, [128, 512], F32) for i in range(4)]
    bufs['dram'] = dict(x=x_d, wqkv=wqkv_d, wout=wout_d, frhs=frhs_d,
                        y=y_d, qfs=qf_d, part=pt_d)

    sem_names = ['LDA', 'LDB', 'LDC', 'LDD', 'SP', 'SQ', 'SY',
                 'TM', 'V', 'S', 'G', 'B0', 'B1', 'B2', 'B3']
    sem_cm = [nc.semaphore(n) for n in sem_names]
    sems = {}
    guards = []
    for n, cm in zip(sem_names, sem_cm):
        h = cm.__enter__()
        sems[n] = h
        guards.append(cm)

    blk_cm = nc.Block()
    blk = blk_cm.__enter__()
    for name in ['sync', 'tensor', 'vector', 'scalar', 'gpsimd']:
        def body(eng, name=name):
            walk(W(nc, name, eng, sems, bufs, seq_t))
        getattr(blk, name)(body)
    blk_cm.__exit__(None, None, None)
    for cmx in reversed(guards):
        cmx.__exit__(None, None, None)
    return nc


def walk(w):
    prologue_const(w)
    prologue_tables(w)
    prologue_frhs(w)
    prologue_wqkv(w)
    for s in range(NSEQ):
        pass1_seq(w, s)
    midphase(w)
    for s in range(NSEQ):
        pass2_seq(w, s)
    if w.on('gpsimd'):
        for s_ in ('SP', 'SQ', 'SY'):
            w.eng.wait_ge(w.sems[s_], w.c[s_])


def prologue_const(w):
    b = w.b
    w.g('iota', b['sG'][:, 0:128], [[1, 128]], channel_multiplier=-1,
        allow_small_or_imprecise_dtypes=True)
    w.drain('gpsimd')
    w.g('tensor_single_scalar', b['identF'][:], b['sG'][:, 0:128], 0.0, Alu.is_equal)
    w.drain('gpsimd')
    w.g('tensor_copy', b['identB'][:], b['identF'][:])
    w.g('memset', b['onesF'][:], 1.0)
    w.g('memset', b['onesR'][:], 1.0)
    w.g('memset', b['ksrow'][:], 0.0)
    w.g('memset', b['vksb'][:], 0.0)
    w.g('memset', b['cEPS'][:], 1e-5)
    w.g('memset', b['cQS'][:], ELU_SCALE)
    ins = w.g('iota', b['tcol'][:], [[128, 32]], channel_multiplier=1,
              allow_small_or_imprecise_dtypes=True)
    w.tok['const'] = w.inc('gpsimd', 'G', 1, ins)
    for e in ('sync', 'tensor', 'vector', 'scalar'):
        w.wait(e, 'G', w.tok['const'])


def _reduce_2pi(w, ang_ap, kbuf, scr, out_buf, add_quarter, add_pio2):
    if add_quarter:
        w.v('tensor_scalar', kbuf, ang_ap, INV2PI, 0.25, Alu.mult, Alu.add)
    else:
        w.v('tensor_scalar', kbuf, ang_ap, INV2PI, None, Alu.mult)
    w.drain('vector')
    w.v('tensor_scalar', scr, kbuf, C32MAGIC, -C32MAGIC, Alu.add, Alu.add)
    w.v('tensor_scalar', out_buf, kbuf, C53MAGIC, -C53MAGIC, Alu.add, Alu.add)
    w.drain('vector')
    w.v('tensor_tensor', out_buf, out_buf, scr, Alu.subtract)
    w.drain('vector')
    w.v('scalar_tensor_tensor', kbuf, out_buf, w.b['stat'][:, 12:13], scr,
        Alu.mult, Alu.add)
    w.drain('vector')
    w.v('scalar_tensor_tensor', out_buf, kbuf, -CW1, ang_ap, Alu.mult, Alu.add)
    w.drain('vector')
    w.v('scalar_tensor_tensor', out_buf, kbuf, -CW2, out_buf, Alu.mult, Alu.add)
    w.drain('vector')
    w.v('scalar_tensor_tensor', out_buf, kbuf, -CW3, out_buf, Alu.mult, Alu.add)
    w.drain('vector')
    if add_pio2:
        w.v('tensor_scalar', out_buf, out_buf, PIO2_HI, PIO2_LO, Alu.add, Alu.add)
        w.drain('vector')
    ins = w.v('tensor_scalar', out_buf, out_buf, PI_CLAMP, -PI_CLAMP,
              Alu.min, Alu.max)
    w.drain('vector')
    return ins


def prologue_tables(w):
    b = w.b
    nct = w.seq_t // 128
    w.wait('vector', 'G', w.tok['const'])
    w.v('tensor_scalar', b['stat'][:, 13:14], b['onesF'][:], T_EPS, -1.0,
        Alu.add, Alu.add)
    w.drain('vector')
    w.v('tensor_scalar', b['stat'][:, 12:13], b['stat'][:, 13:14], T_SCALE,
        None, Alu.mult)
    w.drain('vector')
    for i in range(32):
        asl = _ap(b['sA'], i, [[D, 128], [32, nct]])
        tc = _ap(b['tcol'], 0, [[32, 128], [1, nct]])
        w.v('tensor_scalar', asl, tc, ROPE_F_HL[i][0], None, Alu.mult)
        w.drain('vector')
        w.v('scalar_tensor_tensor', asl, tc, ROPE_F_HL[i][1], asl, Alu.mult, Alu.add)
        w.drain('vector')
    used = nct * 32
    angap = b['sA'][:, 0:used]

    ins = _reduce_2pi(w, angap, b['sB'][:, 0:used], b['sG'][:, 0:used],
                      b['qt'][:, 0:used], True, True)
    tokc = w.inc('vector', 'V', 1, ins)
    w.wait('scalar', 'V', tokc)
    ins = w.s('activation', out=b['ctab'][:, 0:used], in_=b['qt'][:, 0:used], func=AF.Sin)
    tokc2 = w.inc('scalar', 'S', 1, ins)
    w.wait('vector', 'S', tokc2)

    ins = _reduce_2pi(w, angap, b['sB'][:, 0:used], b['sG'][:, 0:used],
                      b['qt'][:, 0:used], False, False)
    toks = w.inc('vector', 'V', 1, ins)
    w.wait('scalar', 'V', toks)
    spos = _ap(b['stab'], 32, [[2048, 128], [64, nct], [1, 32]])
    ins = w.s('activation', out=spos, in_=b['qt'][:, 0:used], func=AF.Sin)
    tok2 = w.inc('scalar', 'S', 1, ins)
    w.wait('vector', 'S', tok2)
    sneg = _ap(b['stab'], 0, [[2048, 128], [64, nct], [1, 32]])
    ins = w.v('tensor_scalar', sneg, spos, -1.0, None, Alu.mult)
    w.tok['tables'] = w.inc('vector', 'V', 1, ins)


def prologue_frhs(w):
    b = w.b
    ar = b['arena']
    for kc in range(NF):
        stg = b['xa'] if kc % 2 == 0 else b['xb']
        if kc >= 2:
            w.wait('sync', 'TM', w.tok['frT%d' % (kc - 2)])
        sem = 'LDA' if kc % 2 == 0 else 'LDB'
        tok = w.dma('sync', sem, stg[:],
                    w.b['dram']['frhs'][kc * 128:(kc + 1) * 128, :])
        w.wait('tensor', sem, tok)
        for grp in range(2):
            def mms(bank, stg=stg, grp=grp):
                last = None
                for j in range(4):
                    ncol = grp * 4 + j
                    last = w.t('transpose', bank[:, j * 128:(j + 1) * 128],
                               stg[:, ncol * 128:(ncol + 1) * 128], b['identF'][:])
                return last
            def evac(bank, kc=kc, grp=grp):
                dst = _ap(ar, (grp * 4) * 1024 + kc * 128, [[24576, 128], [1024, 4], [1, 128]])
                src = _ap(bank.tensor if hasattr(bank, 'tensor') else bank, 0,
                          [[512, 128], [128, 4], [1, 128]])
                return w.s('copy', dst, src)
            w.tok['frT%d' % kc] = w.rotg(mms, 'scalar', evac)
    w.wait('vector', 'S', w.c['S'])
    red_out = _ap(b['stat'], 48, [[64, 128], [1, NF], [0, 1]])
    red_in = _ap(ar, 0, [[24576, 128], [1024, NF], [1, 1024]])
    w.v('reduce_max', red_out, red_in, axis=Ax.X, apply_absolute_value=True)
    w.drain('vector')
    st8 = b['stat'][:, 48:56]
    w.v('tensor_scalar', st8, st8, 1e-12, None, Alu.max)
    w.drain('vector')
    w.v('tensor_scalar', b['stat'][:, 56:64], st8, C127_HI, None, Alu.mult)
    w.drain('vector')
    w.v('scalar_tensor_tensor', st8, st8, C127_LO, b['stat'][:, 56:64],
        Alu.mult, Alu.add)
    w.drain('vector')
    ins = w.v('reciprocal', b['stat'][:, 56:64], st8)
    tokrr = w.inc('vector', 'V', 1, ins)

    tok_qT = 0
    for nc_ in range(NF):
        w.wait('scalar', 'V', tokrr)
        w.wait('scalar', 'V', w.c['V'])
        ins = w.s('activation', out=b['sA'][:], in_=ar[:, nc_ * 1024:(nc_ + 1) * 1024],
                  func=AF.Identity, scale=b['stat'][:, 56 + nc_:57 + nc_])
        toky = w.inc('scalar', 'S', 1, ins)
        w.wait('vector', 'S', toky)
        w.wait('vector', 'TM', tok_qT)
        emit_round(w, b['sA'][:], b['sG'][:], b['sB'][:], b['sG'][:])
        ins = w.v('tensor_scalar', b['qq'][:], b['sG'][:], 127.0, -127.0,
                  Alu.min, Alu.max)
        tokq = w.inc('vector', 'V', 1, ins)
        w.wait('tensor', 'V', tokq)
        for grp in range(2):
            def mms(bank, grp=grp):
                bk = bank.bitcast(BF16)
                last = None
                for j in range(4):
                    f = grp * 4 + j
                    last = w.t('transpose', bk[:, j * 128:(j + 1) * 128],
                               b['qq'][:, f * 128:(f + 1) * 128], b['identB'][:])
                return last
            def evac(bank, nc_=nc_, grp=grp):
                dst = _ap(b['qrhs'], (grp * 4) * 1024 + nc_ * 128,
                          [[8192, 128], [1024, 4], [1, 128]])
                src = _ap(bank.bitcast(BF16), 0, [[1024, 128], [128, 4], [1, 128]])
                return w.s('copy', dst, src)
            tok_qT = w.rotg(mms, 'scalar', evac)
    def mms(bank):
        return w.t('transpose', bank[0:8, 0:128], b['stat'][:, 48:56], b['identF'][:])
    def evac(bank):
        return w.s('copy', b['sG'][0:8, 0:128], bank[0:8, 0:128])
    tokt = w.rotg(mms, 'scalar', evac)
    w.wait('gpsimd', 'S', w.c['S'])
    tokd = w.dma('gpsimd', 'SP', b['ksb'][0:1, 0:D], b['sG'][0:8, 0:128])
    w.wait('tensor', 'SP', tokd)
    for half in range(2):
        def mms(bank, half=half):
            return w.t('matmul', bank[:, 0:512], b['onesR'][0:1, :],
                       b['ksb'][0:1, half * 512:(half + 1) * 512],
                       start=True, stop=True)
        def evac(bank, half=half):
            return w.v('tensor_copy', b['rsb'][:, half * 512:(half + 1) * 512],
                       bank[:, 0:512])
        w.rotg(mms, 'vector', evac)


def prologue_wqkv(w):
    b = w.b
    ar = b['arena']
    for oc in range(24):
        stg = b['xa'] if oc % 2 == 0 else b['xb']
        if oc >= 2:
            w.wait('sync', 'TM', w.tok['wqT%d' % (oc - 2)])
        else:
            w.wait('sync', 'TM', w.c['TM'])
        sem = 'LDA' if oc % 2 == 0 else 'LDB'
        tok = w.dma('sync', sem, stg[:],
                    w.b['dram']['wqkv'][oc * 128:(oc + 1) * 128, :])
        w.wait('tensor', sem, tok)
        for grp in range(2):
            def mms(bank, stg=stg, grp=grp):
                last = None
                for j in range(4):
                    f = grp * 4 + j
                    last = w.t('transpose', bank[:, j * 128:(j + 1) * 128],
                               stg[:, f * 128:(f + 1) * 128], b['identF'][:])
                return last
            def evac(bank, oc=oc, grp=grp):
                dst = _ap(ar, (grp * 4) * 3072 + oc * 128,
                          [[24576, 128], [3072, 4], [1, 128]])
                src = _ap(bank, 0, [[512, 128], [128, 4], [1, 128]])
                return w.s('copy', dst, src)
            w.tok['wqT%d' % oc] = w.rotg(mms, 'scalar', evac)


def pass1_seq(w, s):
    b = w.b
    nt = w.ntiles
    for i in range(nt):
        X = b['xa'] if i % 2 == 0 else b['xb']
        if i == 0:
            w.wait('sync', 'TM', w.c['TM'])
            w.wait('sync', 'V', w.c['V'])
            w.tok['ld_x0'] = w.dma('sync', 'LDA', X[:],
                                   b['dram']['x'][s, 0:128, :])
        ld_tok = w.tok['ld_x0'] if i == 0 else w.tok['ld_x%d' % (i % 2)]
        if i + 1 < nt:
            Xn = b['xb'] if i % 2 == 0 else b['xa']
            w.wait('sync', 'V', w.tok.get('pl_v%d' % ((i + 1) % 2), 0))
            w.tok['ld_x%d' % ((i + 1) % 2)] = w.dma(
                'sync', 'LDA' if (i + 1) % 2 == 0 else 'LDB',
                Xn[:], b['dram']['x'][s, (i + 1) * 128:(i + 2) * 128, :])

        tile_p1(w, s, i, X, ld_tok)
    w.tok['ksrow_done%d' % s] = w.c['V']


def tile_p1(w, s, i, X, ld_tok):
    b = w.b
    st = b['stat']
    ldsem = 'LDA' if i % 2 == 0 else 'LDB'
    w.wait('vector', ldsem, ld_tok)
    w.drain('vector')
    w.v('reduce_sum', st[:, 0:1], X[:], axis=Ax.X)
    w.wait('scalar', ldsem, ld_tok)
    w.wait('scalar', 'SP', w.tok.get('st_pl', 0))
    ins = w.s('activation', out=b['sG'][:], in_=X[:], func=AF.Square,
              accum_out=st[:, 1:2])
    tok_sq = w.inc('scalar', 'S', 1, ins)
    w.drain('vector')
    w.v('tensor_scalar', st[:, 2:3], st[:, 0:1], 1.0 / D, None, Alu.mult)
    w.wait('vector', 'S', tok_sq)
    w.v('tensor_scalar', st[:, 3:4], st[:, 1:2], 1.0 / D, None, Alu.mult)
    w.drain('vector')
    w.v('tensor_tensor', st[:, 4:5], st[:, 2:3], st[:, 2:3], Alu.mult)
    w.drain('vector')
    ins = w.v('tensor_tensor', st[:, 4:5], st[:, 3:4], st[:, 4:5], Alu.subtract)
    tok_var = w.inc('vector', 'V', 1, ins)
    w.wait('scalar', 'V', tok_var)
    ins = w.s('activation', out=st[:, 5:6], in_=st[:, 4:5], func=AF.Sqrt, bias=b['cEPS'][:])
    tok_sd = w.inc('scalar', 'S', 1, ins)
    w.wait('vector', 'S', tok_sd)
    w.v('reciprocal', st[:, 6:7], st[:, 5:6])
    w.drain('vector')
    ins = w.v('scalar_tensor_tensor', st[:, 7:8], st[:, 2:3], -1.0, st[:, 6:7],
              Alu.mult, Alu.mult)
    tok_nm = w.inc('vector', 'V', 1, ins)
    w.wait('scalar', 'V', tok_nm)
    w.wait('scalar', 'TM', w.tok.get('xnT', 0))
    ins = w.s('activation', out=b['xn'][:], in_=X[:], func=AF.Identity,
              scale=st[:, 6:7], bias=st[:, 7:8])
    tok_xn = w.inc('scalar', 'S', 1, ins)

    w.wait('vector', 'S', tok_xn)
    w.v('reduce_max', st[:, 8:9], b['xn'][:], axis=Ax.X, apply_absolute_value=True)
    w.drain('vector')
    w.v('tensor_scalar', st[:, 8:9], st[:, 8:9], 1e-12, None, Alu.max)
    w.drain('vector')
    w.v('tensor_scalar', st[:, 9:10], st[:, 8:9], C127_HI, None, Alu.mult)
    w.drain('vector')
    w.v('scalar_tensor_tensor', st[:, 10:11], st[:, 8:9], C127_LO, st[:, 9:10],
        Alu.mult, Alu.add)
    w.drain('vector')
    ins = w.v('reciprocal', st[:, 11:12], st[:, 10:11])
    tok_rls = w.inc('vector', 'V', 1, ins)
    w.wait('scalar', 'V', tok_rls)
    ins = w.s('activation', out=b['sA'][:], in_=b['xn'][:], func=AF.Identity,
              scale=st[:, 11:12])
    tok_y = w.inc('scalar', 'S', 1, ins)
    w.wait('vector', 'S', tok_y)
    w.wait('vector', 'TM', w.tok.get('qqT', 0))
    w.wait('vector', 'SP', w.tok.get('st_pl', 0))
    emit_round(w, b['sA'][:], b['sG'][:], b['sB'][:], b['sG'][:])
    ins = w.v('tensor_scalar', b['qq'][:], b['sG'][:], 127.0, -127.0, Alu.min, Alu.max)
    tok_qq = w.inc('vector', 'V', 1, ins)

    w.wait('tensor', 'V', tok_qq)
    for grp in range(2):
        def mms(bank, grp=grp):
            bk = bank.bitcast(BF16)
            last = None
            for j in range(4):
                f = grp * 4 + j
                last = w.t('transpose', bk[:, j * 128:(j + 1) * 128],
                           b['qq'][:, f * 128:(f + 1) * 128], b['identB'][:])
            return last
        def evac(bank, grp=grp):
            w.wait('scalar', 'TM', w.tok.get('ffnM', 0))
            dst = _ap(b['qqf'], grp * 512, [[1024, 128], [1, 512]])
            src = _ap(bank.bitcast(BF16), 0, [[1024, 128], [1, 512]])
            return w.s('copy', dst, src)
        tokT = w.rotg(mms, 'scalar', evac)
    w.tok['qqT'] = tokT
    tok_qqf = w.c['S']

    for nb in range(2):
        def mms(bank, nb=nb):
            w.wait('tensor', 'S', tok_qqf)
            last = None
            for f in range(NF):
                last = w.t('matmul', bank[:, 0:512],
                           b['qqf'][:, f * 128:(f + 1) * 128],
                           b['qrhs'][:, f * 1024 + nb * 512: f * 1024 + (nb + 1) * 512],
                           start=(f == 0), stop=(f == NF - 1))
            return last
        def evac(bank, nb=nb):
            w.wait('vector', 'SP', w.tok.get('st_pl', 0))
            w.v('scalar_tensor_tensor', b['sA'][:, nb * 512:(nb + 1) * 512],
                bank[:, 0:512], st[:, 10:11], b['rsb'][:, nb * 512:(nb + 1) * 512],
                Alu.mult, Alu.mult)
            w.drain('vector')
            return w.v('tensor_tensor', b['sG'][:, nb * 512:(nb + 1) * 512],
                       b['sA'][:, nb * 512:(nb + 1) * 512],
                       X[:, nb * 512:(nb + 1) * 512], Alu.add)
        tokF = w.rotg(mms, 'vector', evac)
    w.tok['ffnM'] = tokF
    tok_pl = w.c['V']
    w.tok['pl_v%d' % (i % 2)] = tok_pl
    w.wait('gpsimd', 'V', tok_pl)
    w.tok['st_pl'] = w.dma('gpsimd', 'SP',
                           b['dram']['part'][s, i * 128:(i + 1) * 128, :], b['sG'][:])

    w.wait('tensor', 'S', tok_xn)
    for grp in range(2):
        def mms(bank, grp=grp):
            last = None
            for j in range(4):
                f = grp * 4 + j
                last = w.t('transpose', bank[:, j * 128:(j + 1) * 128],
                           b['xn'][:, f * 128:(f + 1) * 128], b['identF'][:])
            return last
        def evac(bank, grp=grp):
            w.wait('scalar', 'TM', w.tok.get('qkvM', 0))
            dst = _ap(b['xfm'], grp * 512, [[1024, 128], [1, 512]])
            return w.s('copy', dst, bank[:, 0:512])
        tokT = w.rotg(mms, 'scalar', evac)
    w.tok['xnT'] = tokT
    tok_xfm = w.c['S']

    dsts = [b['qt'], b['m1'], b['m2']]
    for pj in range(3):
        for nb in range(2):
            def mms(bank, pj=pj, nb=nb):
                w.wait('tensor', 'S', tok_xfm)
                last = None
                for f in range(NF):
                    rhs = b['arena'][:, f * 3072 + pj * 1024 + nb * 512:
                                     f * 3072 + pj * 1024 + (nb + 1) * 512]
                    last = w.t('matmul', bank[:, 0:512],
                               b['xfm'][:, f * 128:(f + 1) * 128], rhs,
                               start=(f == 0), stop=(f == NF - 1))
                return last
            def evac(bank, pj=pj, nb=nb):
                if pj == 0:
                    w.wait('scalar', 'V', w.tok.get('ropeq', 0))
                elif pj == 1:
                    w.wait('scalar', 'V', w.tok.get('ropek', 0))
                else:
                    w.wait('scalar', 'TM', w.tok.get('kvM', 0))
                return w.s('copy', dsts[pj][:, nb * 512:(nb + 1) * 512],
                           bank[:, 0:512])
            tokP = w.rotg(mms, 'scalar', evac)
    w.tok['qkvM'] = tokP
    tok_proj = w.c['S']

    ct = i
    def rope(src, dst, tok_src_ready, scale):
        w.wait('vector', 'S', tok_src_ready)
        c2 = _ap(b['ctab'], ct * 32, [[D, 128], [0, H], [0, 2], [1, 32]])
        s2 = _ap(b['stab'], ct * 64, [[2048, 128], [0, H], [32, 2], [1, 32]])
        srcv = _ap(src, 0, [[D, 128], [64, H], [32, 2], [1, 32]])
        srcw = _ap(src, 32, [[D, 128], [64, H], [-32, 2], [1, 32]])
        dstv = _ap(dst, 0, [[D, 128], [64, H], [32, 2], [1, 32]])
        sBv = _ap(b['sB'], 0, [[D, 128], [64, H], [32, 2], [1, 32]])
        w.v('tensor_tensor', dstv, srcv, c2, Alu.mult)
        w.v('tensor_tensor', sBv, srcw, s2, Alu.mult)
        w.drain('vector')
        w.v('tensor_tensor', dst[:], dst[:], b['sB'][:], Alu.add)
        w.drain('vector')
        w.v('tensor_scalar', b['sB'][:], dst[:], 0.0, None, Alu.min)
        w.drain('vector')
        ins = w.v('tensor_scalar', dst[:], dst[:], scale, 0.0, Alu.mult, Alu.max)
        tok_mx = w.inc('vector', 'V', 1, ins)
        w.wait('scalar', 'V', tok_mx)
        ins = w.s('activation', out=b['sB'][:], in_=b['sB'][:], func=AF.Exp,
                  scale=(b['cQS'][:] if scale != 1.0 else 1.0))
        tok_e = w.inc('scalar', 'S', 1, ins)
        w.wait('vector', 'S', tok_e)
        w.drain('vector')
        ins = w.v('tensor_tensor', dst[:], dst[:], b['sB'][:], Alu.add)
        return w.inc('vector', 'V', 1, ins)

    w.wait('vector', 'TM', w.tok.get('kvM', 0))
    tok_kf = rope(b['m1'], b['kr'], tok_proj, 1.0)
    w.tok['ropek'] = tok_kf
    w.wait('vector', 'SQ', w.tok.get('st_qf', 0))
    tok_qf = rope(b['qt'], b['qr'], tok_proj, ELU_SCALE)
    w.tok['ropeq'] = tok_qf

    w.wait('gpsimd', 'V', tok_qf)
    w.tok['st_qf'] = w.dma('gpsimd', 'SQ',
                           b['dram']['qfs'][s, i * 128:(i + 1) * 128, :], b['qr'][:])

    nt = w.ntiles
    w.wait('tensor', 'V', tok_kf)
    last = None
    for p in range(NPAIR):
        bank = b['pkv'][s * 2 + p // 4]
        first = (i == 0) and (p % 4 == 0)
        lastm = (i == nt - 1) and (p % 4 == 3)
        last = w.t('matmul', bank[:, (p % 4) * 128:(p % 4 + 1) * 128],
                   b['m2'][:, p * 128:(p + 1) * 128],
                   b['kr'][:, p * 128:(p + 1) * 128],
                   start=first, stop=lastm)
    tok_kv1 = w.inc('tensor', 'TM', 1, last)
    tokK = 0
    for half in range(2):
        def mms(bank, half=half):
            return w.t('matmul', bank[0:1, 0:512], b['onesF'][:, 0:1],
                       b['kr'][:, half * 512:(half + 1) * 512], start=True, stop=True)
        def evac(bank, half=half, s=s):
            dst = b['ksrow'][0:1, s * D + half * 512: s * D + (half + 1) * 512]
            w.drain('vector')
            return w.v('tensor_tensor', dst, dst, bank[0:1, 0:512], Alu.add)
        tokK = w.rotg(mms, 'vector', evac)
    w.tok['kvM'] = max(tok_kv1, tokK)


def midphase(w):
    b = w.b
    ar = b['arena']
    for oc in range(NF):
        stg = b['xa'] if oc % 2 == 0 else b['xb']
        w.wait('sync', 'V', w.c['V'])
        w.wait('sync', 'TM', w.c['TM'])
        sem = 'LDA' if oc % 2 == 0 else 'LDB'
        tok = w.dma('sync', sem, stg[:],
                    b['dram']['wout'][oc * 128:(oc + 1) * 128, :])
        w.wait('tensor', sem, tok)
        for grp in range(2):
            def mms(bank, stg=stg, grp=grp):
                last = None
                for j in range(4):
                    f = grp * 4 + j
                    last = w.t('transpose', bank[:, j * 128:(j + 1) * 128],
                               stg[:, f * 128:(f + 1) * 128], b['identF'][:])
                return last
            def evac(bank, oc=oc, grp=grp):
                dst = _ap(ar, (grp * 4) * 1024 + oc * 128,
                          [[24576, 128], [1024, 4], [1, 128]])
                src = _ap(bank, 0, [[512, 128], [128, 4], [1, 128]])
                return w.s('copy', dst, src)
            w.rotg(mms, 'scalar', evac)
    tok_wT = w.c['S']

    tok_m = 0
    for s in range(NSEQ):
        w.wait('vector', 'TM', max(w.tok['kvM'], tok_m))
        last = None
        for p in range(NPAIR):
            bank = b['pkv'][s * 2 + p // 4]
            base = p * 128
            last = w.v('tensor_copy', b['vksb'][0:64, base:base + 64],
                       bank[0:64, (p % 4) * 128:(p % 4) * 128 + 64])
            last = w.v('tensor_copy', b['vksb'][64:128, base + 64:base + 128],
                       bank[64:128, (p % 4) * 128 + 64:(p % 4) * 128 + 128])
        tok_vk = w.inc('vector', 'V', 1, last)
        for p in range(NPAIR):
            for nb in range(2):
                def mms(bank, s=s, p=p, nb=nb):
                    w.wait('tensor', 'S', tok_wT)
                    w.wait('tensor', 'V', tok_vk)
                    return w.t('matmul', bank[:, 0:512],
                               b['vksb'][:, p * 128:(p + 1) * 128],
                               ar[:, p * 1024 + nb * 512: p * 1024 + (nb + 1) * 512],
                               start=True, stop=True)
                def evac(bank, s=s, p=p, nb=nb):
                    dst = ar[:, 8192 + (s * NPAIR + p) * 1024 + nb * 512:
                             8192 + (s * NPAIR + p) * 1024 + (nb + 1) * 512]
                    return w.s('copy', dst, bank[:, 0:512])
                tok_m = w.rotg(mms, 'scalar', evac)


def pass2_seq(w, s):
    b = w.b
    nt = w.ntiles
    w.wait('sync', 'SQ', w.c['SQ'])
    w.wait('sync', 'SP', w.c['SP'])
    for half in range(2):
        def mms(bank, half=half):
            w.wait('tensor', 'V', w.tok['ksrow_done%d' % s])
            return w.t('matmul', bank[:, 0:512], b['onesR'][0:1, :],
                       b['ksrow'][0:1, s * D + half * 512: s * D + (half + 1) * 512],
                       start=True, stop=True)
        def evac(bank, half=half):
            w.drain('vector')
            return w.v('tensor_copy', b['ksb'][:, half * 512:(half + 1) * 512],
                       bank[:, 0:512])
        w.rotg(mms, 'vector', evac)
    for j in range(nt):
        QF = b['xa'] if j % 2 == 0 else b['xb']
        PT = b['m1'] if j % 2 == 0 else b['m2']
        if j == 0:
            w.wait('sync', 'V', w.c['V'])
            w.wait('sync', 'S', w.c['S'])
            w.wait('sync', 'TM', w.c['TM'])
            w.tok['ld_qf0'] = w.dma('sync', 'LDA', QF[:], b['dram']['qfs'][s, 0:128, :])
            w.tok['ld_pt0'] = w.dma('sync', 'LDC', PT[:], b['dram']['part'][s, 0:128, :])
            ldq, ldp = w.tok['ld_qf0'], w.tok['ld_pt0']
        else:
            ldq, ldp = w.tok['ld_qf%d' % (j % 2)], w.tok['ld_pt%d' % (j % 2)]
        if j + 1 < nt:
            QFn = b['xb'] if j % 2 == 0 else b['xa']
            PTn = b['m2'] if j % 2 == 0 else b['m1']
            w.wait('sync', 'V', w.tok.get('p2den%d' % ((j + 1) % 2), 0))
            w.wait('sync', 'V', w.tok.get('p2ev%d' % ((j + 1) % 2), 0))
            w.tok['ld_qf%d' % ((j + 1) % 2)] = w.dma(
                'sync', 'LDA' if (j + 1) % 2 == 0 else 'LDB',
                QFn[:], b['dram']['qfs'][s, (j + 1) * 128:(j + 2) * 128, :])
            w.tok['ld_pt%d' % ((j + 1) % 2)] = w.dma(
                'sync', 'LDC' if (j + 1) % 2 == 0 else 'LDD',
                PTn[:], b['dram']['part'][s, (j + 1) * 128:(j + 2) * 128, :])
        tile_p2(w, s, j, QF, PT, ldq, ldp)


def tile_p2(w, s, j, QF, PT, ldq, ldp):
    b = w.b
    st = b['stat']
    qsem = 'LDA' if j % 2 == 0 else 'LDB'
    psem = 'LDC' if j % 2 == 0 else 'LDD'
    w.wait('vector', qsem, ldq)
    w.v('tensor_tensor', b['sA'][:], QF[:], b['ksb'][:], Alu.mult)
    w.drain('vector')
    dred = _ap(st, 16, [[64, 128], [1, H], [0, 1]])
    din = _ap(b['sA'], 0, [[D, 128], [64, H], [1, 64]])
    w.v('reduce_sum', dred, din, axis=Ax.X)
    w.drain('vector')
    w.v('tensor_scalar', st[:, 16:32], st[:, 16:32], 1e-6, None, Alu.max)
    w.drain('vector')
    w.v('reciprocal', st[:, 32:48], st[:, 16:32])
    w.drain('vector')
    w.wait('vector', 'TM', w.tok.get('qsT', 0))
    ins = None
    for h in range(H):
        ins = w.v('tensor_scalar', b['xn'][:, h * 64:(h + 1) * 64],
                  QF[:, h * 64:(h + 1) * 64], st[:, 32 + h:33 + h], None, Alu.mult)
    tok_qs = w.inc('vector', 'V', 1, ins)
    w.tok['p2den%d' % (j % 2)] = tok_qs

    w.wait('tensor', 'V', tok_qs)
    for grp in range(2):
        def mms(bank, grp=grp):
            last = None
            for jj in range(4):
                f = grp * 4 + jj
                last = w.t('transpose', bank[:, jj * 128:(jj + 1) * 128],
                           b['xn'][:, f * 128:(f + 1) * 128], b['identF'][:])
            return last
        def evac(bank, grp=grp):
            w.wait('scalar', 'TM', w.tok.get('mainM', 0))
            dst = _ap(b['xfm'], grp * 512, [[1024, 128], [1, 512]])
            return w.s('copy', dst, bank[:, 0:512])
        tokT = w.rotg(mms, 'scalar', evac)
    w.tok['qsT'] = tokT
    tok_qfm = w.c['S']

    for nb in range(2):
        def mms(bank, nb=nb):
            w.wait('tensor', 'S', tok_qfm)
            last = None
            for p in range(NPAIR):
                rhs = b['arena'][:, 8192 + (s * NPAIR + p) * 1024 + nb * 512:
                                 8192 + (s * NPAIR + p) * 1024 + (nb + 1) * 512]
                last = w.t('matmul', bank[:, 0:512],
                           b['xfm'][:, p * 128:(p + 1) * 128], rhs,
                           start=(p == 0), stop=(p == NPAIR - 1))
            return last
        def evac(bank, nb=nb):
            w.wait('vector', psem, ldp)
            w.wait('vector', 'SY', w.tok.get('st_y', 0))
            w.drain('vector')
            return w.v('scalar_tensor_tensor', b['qt'][:, nb * 512:(nb + 1) * 512],
                       bank[:, 0:512], 1.0, PT[:, nb * 512:(nb + 1) * 512],
                       Alu.mult, Alu.add)
        tokM = w.rotg(mms, 'vector', evac)
    w.tok['mainM'] = tokM
    tok_out = w.c['V']
    w.tok['p2ev%d' % (j % 2)] = tok_out

    w.wait('gpsimd', 'V', tok_out)
    w.tok['st_y'] = w.dma('gpsimd', 'SY',
                          b['dram']['y'][s, j * 128:(j + 1) * 128, :], b['qt'][:])


def kernel(L0_attn_x, L0_attn_w_qkv, L0_attn_w_out, L0_ffn_rhs):
    x = np.ascontiguousarray(np.asarray(L0_attn_x, dtype=np.float32))
    wqkv = np.ascontiguousarray(np.asarray(L0_attn_w_qkv, dtype=np.float32))
    wout = np.ascontiguousarray(np.asarray(L0_attn_w_out, dtype=np.float32))
    frhs = np.ascontiguousarray(np.asarray(L0_ffn_rhs, dtype=np.float32))
    B, T, _ = x.shape
    n_cores = 8
    per = B // n_cores
    nc = build_nc(seq_t=T)
    in_maps = []
    for c in range(n_cores):
        in_maps.append({
            "x": x[c * per:(c + 1) * per],
            "wqkv": wqkv,
            "wout": wout,
            "frhs": frhs,
        })
    res = run_bass_kernel_spmd(nc, in_maps, core_ids=list(range(n_cores)))
    return np.concatenate([r["y"] for r in res.results], axis=0)


# version 03
# speedup vs baseline: 1.0815x; 1.0815x over previous
import math
import numpy as np

import concourse.bass as bass
import concourse.mybir as mybir
from concourse.bass_utils import run_bass_kernel_spmd

F32 = mybir.dt.float32
BF16 = mybir.dt.bfloat16
Alu = mybir.AluOpType
AF = mybir.ActivationFunctionType
Ax = mybir.AxisListType

D = 1024
H = 16
HD = 64
NF = 8
NPAIR = 8
NSEQ = 2

def _hl(v):
    hi = np.float32(v)
    lo = np.float32(v - np.float64(hi))
    return float(hi), float(lo)

C127_HI, C127_LO = _hl(1.0 / 127.0)
INV2PI = float(np.float32(1.0 / (2.0 * math.pi)))
def _rnd_bits(v, bits):
    m, e = math.frexp(v)
    scale = 2.0 ** (bits - e)
    return math.floor(v * scale + 0.5) / scale
TWO_PI = 2.0 * math.pi
CW1 = float(np.float32(_rnd_bits(TWO_PI, 14)))
CW2 = float(np.float32(_rnd_bits(TWO_PI - CW1, 14)))
CW3 = float(np.float32(TWO_PI - CW1 - CW2))
PIO2_HI, PIO2_LO = _hl(math.pi / 2.0)
PI_CLAMP = float(np.nextafter(np.float32(math.pi), np.float32(0)))
ROPE_F = [10000.0 ** (-i / 32.0) for i in range(32)]
ROPE_F_HL = [_hl(f) for f in ROPE_F]
ELU_SCALE = 0.125
C32MAGIC = float(np.float32(1.5 * 2.0 ** 23))
C53MAGIC = float(np.float32(1.5 * 2.0 ** 52))
T_EPS = float(np.float32(2.0 ** -30))
T_SCALE = float(np.float32(2.0 ** 30))


def emit_round(w, y_ap, out_ap, m_ap, u_ap):
    b = w.b
    w.v('tensor_scalar', m_ap, y_ap, C32MAGIC, -C32MAGIC, Alu.add, Alu.add)
    w.v('tensor_scalar', u_ap, y_ap, C53MAGIC, -C53MAGIC, Alu.add, Alu.add)
    w.drain('vector')
    w.v('tensor_tensor', u_ap, u_ap, m_ap, Alu.subtract)
    w.drain('vector')
    ins = w.v('scalar_tensor_tensor', out_ap, u_ap, b['stat'][:, 12:13], m_ap,
              Alu.mult, Alu.add)
    w.drain('vector')
    return ins


class W:

    def __init__(self, nc, eng_name, eng, sems, bufs, seq_t):
        self.nc = nc
        self.en = eng_name
        self.eng = eng
        self.sems = sems
        self.b = bufs
        self.seq_t = seq_t
        self.ntiles = seq_t // 128
        self.c = {k: 0 for k in sems}
        self.rg_idx = 0
        self.bfc = [0, 0, 0, 0]
        self.tok = {}

    def on(self, name):
        return self.en == name

    def _e(self, name, meth, *a, **k):
        if self.en == name:
            return getattr(self.eng, meth)(*a, **k)
        return None

    def t(self, meth, *a, **k):
        return self._e('tensor', meth, *a, **k)

    def v(self, meth, *a, **k):
        return self._e('vector', meth, *a, **k)

    def s(self, meth, *a, **k):
        return self._e('scalar', meth, *a, **k)

    def g(self, meth, *a, **k):
        return self._e('gpsimd', meth, *a, **k)

    def drain(self, name):
        if self.en == name:
            self.eng.drain()

    def wait(self, name, sem, val):
        if val > 0 and self.en == name:
            self.eng.wait_ge(self.sems[sem], val)

    def inc(self, name, sem, n, instr):
        self.c[sem] += n
        if self.en == name and instr is not None:
            instr.then_inc(self.sems[sem], n)
            self.eng.wait_ge(self.sems[sem], self.c[sem])
        return self.c[sem]

    def dma(self, qeng, sem, out_ap, in_ap):
        self.c[sem] += 16
        if self.en == qeng:
            self.eng.dma_start(out=out_ap, in_=in_ap).then_inc(self.sems[sem], 16)
        return self.c[sem]

    def rotg(self, mms_fn, evac_eng, evac_fn):
        bi = self.rg_idx % 4
        self.rg_idx += 1
        bank = self.b['pw'][bi]
        self.wait('tensor', 'B%d' % bi, self.bfc[bi])
        last = mms_fn(bank)
        tok = self.inc('tensor', 'TM', 1, last)
        self.wait(evac_eng, 'TM', tok)
        ev = evac_fn(bank)
        self.bfc[bi] += 1
        self.inc(evac_eng, 'B%d' % bi, 1, ev)
        sem = {'scalar': 'S', 'vector': 'V'}[evac_eng]
        d = self._e(evac_eng, 'drain')
        self.inc(evac_eng, sem, 1, d)
        return tok


def _ap(t, off, dims):
    return bass.AP(t, off, dims)


def build_nc(seq_t=4096):
    nc = bass.Bass("TRN2", target_bir_lowering=False)
    T = seq_t
    x_d = nc.dram_tensor("x", [NSEQ, T, D], F32, kind="ExternalInput")
    wqkv_d = nc.dram_tensor("wqkv", [3 * D, D], F32, kind="ExternalInput")
    wout_d = nc.dram_tensor("wout", [D, D], F32, kind="ExternalInput")
    frhs_d = nc.dram_tensor("frhs", [D, D], F32, kind="ExternalInput")
    y_d = nc.dram_tensor("y", [NSEQ, T, D], F32, kind="ExternalOutput")
    qf_d = nc.dram_tensor("qfs", [NSEQ, T, D], F32, kind="Internal")
    pt_d = nc.dram_tensor("part", [NSEQ, T, D], F32, kind="Internal")

    al = nc.alloc_sbuf_tensor
    bufs = dict(
        arena=al("arena", [128, 24576], F32),
        qrhs=al("qrhs", [128, 8192], BF16),
        rsb=al("rsb", [128, D], F32),
        ctab=al("ctab", [128, D], F32),
        stab=al("stab", [128, 2048], F32),
        identF=al("identF", [128, 128], F32),
        identB=al("identB", [128, 128], BF16),
        onesF=al("onesF", [128, 1], F32),
        onesR=al("onesR", [1, 128], F32),
        ksrow=al("ksrow", [1, 2 * D], F32),
        ksb=al("ksb", [128, D], F32),
        vksb=al("vksb", [128, D], F32),
        xa=al("xa", [128, D], F32),
        xb=al("xb", [128, D], F32),
        m1=al("m1", [128, D], F32),
        m2=al("m2", [128, D], F32),
        xn=al("xn", [128, D], F32),
        qt=al("qt", [128, D], F32),
        kr=al("kr", [128, D], F32),
        qr=al("qr", [128, D], F32),
        sA=al("sA", [128, D], F32),
        sB=al("sB", [128, D], F32),
        sG=al("sG", [128, D], F32),
        xfm=al("xfm", [128, D], F32),
        qq=al("qq", [128, D], BF16),
        qqf=al("qqf", [128, D], BF16),
        stat=al("stat", [128, 64], F32),
        tcol=al("tcol", [128, 32], F32),
        cEPS=al("cEPS", [128, 1], F32),
        cQS=al("cQS", [128, 1], F32),
    )
    ap = nc.alloc_psum_tensor
    bufs['pw'] = [ap("pw%d" % i, [128, 512], F32) for i in range(4)]
    bufs['pkv'] = [ap("pkv%d" % i, [128, 512], F32) for i in range(4)]
    bufs['dram'] = dict(x=x_d, wqkv=wqkv_d, wout=wout_d, frhs=frhs_d,
                        y=y_d, qfs=qf_d, part=pt_d)

    sem_names = ['LDA', 'LDB', 'LDC', 'LDD', 'SP', 'SQ', 'SY',
                 'TM', 'V', 'S', 'G', 'B0', 'B1', 'B2', 'B3']
    sem_cm = [nc.semaphore(n) for n in sem_names]
    sems = {}
    guards = []
    for n, cm in zip(sem_names, sem_cm):
        h = cm.__enter__()
        sems[n] = h
        guards.append(cm)

    blk_cm = nc.Block()
    blk = blk_cm.__enter__()
    for name in ['sync', 'tensor', 'vector', 'scalar', 'gpsimd']:
        def body(eng, name=name):
            walk(W(nc, name, eng, sems, bufs, seq_t))
        getattr(blk, name)(body)
    blk_cm.__exit__(None, None, None)
    for cmx in reversed(guards):
        cmx.__exit__(None, None, None)
    return nc


def walk(w):
    prologue_const(w)
    prologue_tables(w)
    prologue_frhs(w)
    prologue_wqkv(w)
    for s in range(NSEQ):
        pass1_seq(w, s)
    midphase(w)
    for s in range(NSEQ):
        pass2_seq(w, s)
    if w.on('gpsimd'):
        for s_ in ('SP', 'SQ', 'SY'):
            w.eng.wait_ge(w.sems[s_], w.c[s_])


def prologue_const(w):
    b = w.b
    w.g('iota', b['sG'][:, 0:128], [[1, 128]], channel_multiplier=-1,
        allow_small_or_imprecise_dtypes=True)
    w.drain('gpsimd')
    w.g('tensor_single_scalar', b['identF'][:], b['sG'][:, 0:128], 0.0, Alu.is_equal)
    w.drain('gpsimd')
    w.g('tensor_copy', b['identB'][:], b['identF'][:])
    w.g('memset', b['onesF'][:], 1.0)
    w.g('memset', b['onesR'][:], 1.0)
    w.g('memset', b['ksrow'][:], 0.0)
    w.g('memset', b['vksb'][:], 0.0)
    w.g('memset', b['cEPS'][:], 1e-5)
    w.g('memset', b['cQS'][:], ELU_SCALE)
    ins = w.g('iota', b['tcol'][:], [[128, 32]], channel_multiplier=1,
              allow_small_or_imprecise_dtypes=True)
    w.tok['const'] = w.inc('gpsimd', 'G', 1, ins)
    for e in ('sync', 'tensor', 'vector', 'scalar'):
        w.wait(e, 'G', w.tok['const'])


def _reduce_2pi(w, ang_ap, kbuf, scr, out_buf, add_quarter, add_pio2):
    if add_quarter:
        w.v('tensor_scalar', kbuf, ang_ap, INV2PI, 0.25, Alu.mult, Alu.add)
    else:
        w.v('tensor_scalar', kbuf, ang_ap, INV2PI, None, Alu.mult)
    w.drain('vector')
    w.v('tensor_scalar', scr, kbuf, C32MAGIC, -C32MAGIC, Alu.add, Alu.add)
    w.v('tensor_scalar', out_buf, kbuf, C53MAGIC, -C53MAGIC, Alu.add, Alu.add)
    w.drain('vector')
    w.v('tensor_tensor', out_buf, out_buf, scr, Alu.subtract)
    w.drain('vector')
    w.v('scalar_tensor_tensor', kbuf, out_buf, w.b['stat'][:, 12:13], scr,
        Alu.mult, Alu.add)
    w.drain('vector')
    w.v('scalar_tensor_tensor', out_buf, kbuf, -CW1, ang_ap, Alu.mult, Alu.add)
    w.drain('vector')
    w.v('scalar_tensor_tensor', out_buf, kbuf, -CW2, out_buf, Alu.mult, Alu.add)
    w.drain('vector')
    w.v('scalar_tensor_tensor', out_buf, kbuf, -CW3, out_buf, Alu.mult, Alu.add)
    w.drain('vector')
    if add_pio2:
        w.v('tensor_scalar', out_buf, out_buf, PIO2_HI, PIO2_LO, Alu.add, Alu.add)
        w.drain('vector')
    ins = w.v('tensor_scalar', out_buf, out_buf, PI_CLAMP, -PI_CLAMP,
              Alu.min, Alu.max)
    w.drain('vector')
    return ins


def prologue_tables(w):
    b = w.b
    nct = w.seq_t // 128
    w.wait('vector', 'G', w.tok['const'])
    w.v('tensor_scalar', b['stat'][:, 13:14], b['onesF'][:], T_EPS, -1.0,
        Alu.add, Alu.add)
    w.drain('vector')
    w.v('tensor_scalar', b['stat'][:, 12:13], b['stat'][:, 13:14], T_SCALE,
        None, Alu.mult)
    w.drain('vector')
    for i in range(32):
        asl = _ap(b['sA'], i, [[D, 128], [32, nct]])
        tc = _ap(b['tcol'], 0, [[32, 128], [1, nct]])
        w.v('tensor_scalar', asl, tc, ROPE_F_HL[i][0], None, Alu.mult)
        w.drain('vector')
        w.v('scalar_tensor_tensor', asl, tc, ROPE_F_HL[i][1], asl, Alu.mult, Alu.add)
        w.drain('vector')
    used = nct * 32
    angap = b['sA'][:, 0:used]

    ins = _reduce_2pi(w, angap, b['sB'][:, 0:used], b['sG'][:, 0:used],
                      b['qt'][:, 0:used], True, True)
    tokc = w.inc('vector', 'V', 1, ins)
    w.wait('scalar', 'V', tokc)
    ins = w.s('activation', out=b['ctab'][:, 0:used], in_=b['qt'][:, 0:used], func=AF.Sin)
    tokc2 = w.inc('scalar', 'S', 1, ins)
    w.wait('vector', 'S', tokc2)

    ins = _reduce_2pi(w, angap, b['sB'][:, 0:used], b['sG'][:, 0:used],
                      b['qt'][:, 0:used], False, False)
    toks = w.inc('vector', 'V', 1, ins)
    w.wait('scalar', 'V', toks)
    spos = _ap(b['stab'], 32, [[2048, 128], [64, nct], [1, 32]])
    ins = w.s('activation', out=spos, in_=b['qt'][:, 0:used], func=AF.Sin)
    tok2 = w.inc('scalar', 'S', 1, ins)
    w.wait('vector', 'S', tok2)
    sneg = _ap(b['stab'], 0, [[2048, 128], [64, nct], [1, 32]])
    ins = w.v('tensor_scalar', sneg, spos, -1.0, None, Alu.mult)
    w.tok['tables'] = w.inc('vector', 'V', 1, ins)


def prologue_frhs(w):
    b = w.b
    ar = b['arena']
    for kc in range(NF):
        stg = b['xa'] if kc % 2 == 0 else b['xb']
        if kc >= 2:
            w.wait('sync', 'TM', w.tok['frT%d' % (kc - 2)])
        sem = 'LDA' if kc % 2 == 0 else 'LDB'
        tok = w.dma('sync', sem, stg[:],
                    w.b['dram']['frhs'][kc * 128:(kc + 1) * 128, :])
        w.wait('tensor', sem, tok)
        for grp in range(2):
            def mms(bank, stg=stg, grp=grp):
                last = None
                for j in range(4):
                    ncol = grp * 4 + j
                    last = w.t('transpose', bank[:, j * 128:(j + 1) * 128],
                               stg[:, ncol * 128:(ncol + 1) * 128], b['identF'][:])
                return last
            def evac(bank, kc=kc, grp=grp):
                dst = _ap(ar, (grp * 4) * 1024 + kc * 128, [[24576, 128], [1024, 4], [1, 128]])
                src = _ap(bank.tensor if hasattr(bank, 'tensor') else bank, 0,
                          [[512, 128], [128, 4], [1, 128]])
                return w.s('copy', dst, src)
            w.tok['frT%d' % kc] = w.rotg(mms, 'scalar', evac)
    w.wait('vector', 'S', w.c['S'])
    red_out = _ap(b['stat'], 48, [[64, 128], [1, NF], [0, 1]])
    red_in = _ap(ar, 0, [[24576, 128], [1024, NF], [1, 1024]])
    w.v('reduce_max', red_out, red_in, axis=Ax.X, apply_absolute_value=True)
    w.drain('vector')
    st8 = b['stat'][:, 48:56]
    w.v('tensor_scalar', st8, st8, 1e-12, None, Alu.max)
    w.drain('vector')
    w.v('tensor_scalar', b['stat'][:, 56:64], st8, C127_HI, None, Alu.mult)
    w.drain('vector')
    w.v('scalar_tensor_tensor', st8, st8, C127_LO, b['stat'][:, 56:64],
        Alu.mult, Alu.add)
    w.drain('vector')
    ins = w.v('reciprocal', b['stat'][:, 56:64], st8)
    tokrr = w.inc('vector', 'V', 1, ins)

    tok_qT = 0
    for nc_ in range(NF):
        w.wait('scalar', 'V', tokrr)
        w.wait('scalar', 'V', w.c['V'])
        ins = w.s('activation', out=b['sA'][:], in_=ar[:, nc_ * 1024:(nc_ + 1) * 1024],
                  func=AF.Identity, scale=b['stat'][:, 56 + nc_:57 + nc_])
        toky = w.inc('scalar', 'S', 1, ins)
        w.wait('vector', 'S', toky)
        w.wait('vector', 'TM', tok_qT)
        emit_round(w, b['sA'][:], b['sG'][:], b['sB'][:], b['sG'][:])
        ins = w.v('tensor_scalar', b['sG'][:], b['sG'][:], 127.0, -127.0,
                  Alu.min, Alu.max)
        tokq = w.inc('vector', 'V', 1, ins)
        w.wait('tensor', 'V', tokq)
        for grp in range(2):
            def mms(bank, grp=grp):
                last = None
                for j in range(4):
                    f = grp * 4 + j
                    last = w.t('transpose', bank[:, j * 128:(j + 1) * 128],
                               b['sG'][:, f * 128:(f + 1) * 128], b['identF'][:])
                return last
            def evac(bank, nc_=nc_, grp=grp):
                dst = _ap(b['qrhs'], (grp * 4) * 1024 + nc_ * 128,
                          [[8192, 128], [1024, 4], [1, 128]])
                src = _ap(bank, 0, [[512, 128], [128, 4], [1, 128]])
                return w.s('copy', dst, src)
            tok_qT = w.rotg(mms, 'scalar', evac)
    def mms(bank):
        return w.t('transpose', bank[0:8, 0:128], b['stat'][:, 48:56], b['identF'][:])
    def evac(bank):
        return w.s('copy', b['sG'][0:8, 0:128], bank[0:8, 0:128])
    tokt = w.rotg(mms, 'scalar', evac)
    w.wait('gpsimd', 'S', w.c['S'])
    tokd = w.dma('gpsimd', 'SP', b['ksb'][0:1, 0:D], b['sG'][0:8, 0:128])
    w.wait('tensor', 'SP', tokd)
    for half in range(2):
        def mms(bank, half=half):
            return w.t('matmul', bank[:, 0:512], b['onesR'][0:1, :],
                       b['ksb'][0:1, half * 512:(half + 1) * 512],
                       start=True, stop=True)
        def evac(bank, half=half):
            return w.v('tensor_copy', b['rsb'][:, half * 512:(half + 1) * 512],
                       bank[:, 0:512])
        w.rotg(mms, 'vector', evac)


def prologue_wqkv(w):
    b = w.b
    ar = b['arena']
    for oc in range(24):
        stg = b['xa'] if oc % 2 == 0 else b['xb']
        if oc >= 2:
            w.wait('sync', 'TM', w.tok['wqT%d' % (oc - 2)])
        else:
            w.wait('sync', 'TM', w.c['TM'])
        sem = 'LDA' if oc % 2 == 0 else 'LDB'
        tok = w.dma('sync', sem, stg[:],
                    w.b['dram']['wqkv'][oc * 128:(oc + 1) * 128, :])
        w.wait('tensor', sem, tok)
        for grp in range(2):
            def mms(bank, stg=stg, grp=grp):
                last = None
                for j in range(4):
                    f = grp * 4 + j
                    last = w.t('transpose', bank[:, j * 128:(j + 1) * 128],
                               stg[:, f * 128:(f + 1) * 128], b['identF'][:])
                return last
            def evac(bank, oc=oc, grp=grp):
                dst = _ap(ar, (grp * 4) * 3072 + oc * 128,
                          [[24576, 128], [3072, 4], [1, 128]])
                src = _ap(bank, 0, [[512, 128], [128, 4], [1, 128]])
                return w.s('copy', dst, src)
            w.tok['wqT%d' % oc] = w.rotg(mms, 'scalar', evac)


def pass1_seq(w, s):
    b = w.b
    nt = w.ntiles
    for i in range(nt):
        X = b['xa'] if i % 2 == 0 else b['xb']
        if i == 0:
            w.wait('sync', 'TM', w.c['TM'])
            w.wait('sync', 'V', w.c['V'])
            w.tok['ld_x0'] = w.dma('sync', 'LDA', X[:],
                                   b['dram']['x'][s, 0:128, :])
        ld_tok = w.tok['ld_x0'] if i == 0 else w.tok['ld_x%d' % (i % 2)]
        if i + 1 < nt:
            Xn = b['xb'] if i % 2 == 0 else b['xa']
            w.wait('sync', 'V', w.tok.get('pl_v%d' % ((i + 1) % 2), 0))
            w.tok['ld_x%d' % ((i + 1) % 2)] = w.dma(
                'sync', 'LDA' if (i + 1) % 2 == 0 else 'LDB',
                Xn[:], b['dram']['x'][s, (i + 1) * 128:(i + 2) * 128, :])

        tile_p1(w, s, i, X, ld_tok)
    w.tok['ksrow_done%d' % s] = w.c['V']


def tile_p1(w, s, i, X, ld_tok):
    b = w.b
    st = b['stat']
    ldsem = 'LDA' if i % 2 == 0 else 'LDB'
    w.wait('vector', ldsem, ld_tok)
    w.drain('vector')
    w.v('reduce_sum', st[:, 0:1], X[:], axis=Ax.X)
    w.wait('scalar', ldsem, ld_tok)
    w.wait('scalar', 'SP', w.tok.get('st_pl', 0))
    ins = w.s('activation', out=b['sG'][:], in_=X[:], func=AF.Square,
              accum_out=st[:, 1:2])
    tok_sq = w.inc('scalar', 'S', 1, ins)
    w.drain('vector')
    w.v('tensor_scalar', st[:, 2:3], st[:, 0:1], 1.0 / D, None, Alu.mult)
    w.wait('vector', 'S', tok_sq)
    w.v('tensor_scalar', st[:, 3:4], st[:, 1:2], 1.0 / D, None, Alu.mult)
    w.drain('vector')
    w.v('tensor_tensor', st[:, 4:5], st[:, 2:3], st[:, 2:3], Alu.mult)
    w.drain('vector')
    ins = w.v('tensor_tensor', st[:, 4:5], st[:, 3:4], st[:, 4:5], Alu.subtract)
    tok_var = w.inc('vector', 'V', 1, ins)
    w.wait('scalar', 'V', tok_var)
    ins = w.s('activation', out=st[:, 5:6], in_=st[:, 4:5], func=AF.Sqrt, bias=b['cEPS'][:])
    tok_sd = w.inc('scalar', 'S', 1, ins)
    w.wait('vector', 'S', tok_sd)
    w.v('reciprocal', st[:, 6:7], st[:, 5:6])
    w.drain('vector')
    ins = w.v('scalar_tensor_tensor', st[:, 7:8], st[:, 2:3], -1.0, st[:, 6:7],
              Alu.mult, Alu.mult)
    tok_nm = w.inc('vector', 'V', 1, ins)
    w.wait('scalar', 'V', tok_nm)
    w.wait('scalar', 'TM', w.tok.get('xnT', 0))
    ins = w.s('activation', out=b['xn'][:], in_=X[:], func=AF.Identity,
              scale=st[:, 6:7], bias=st[:, 7:8])
    tok_xn = w.inc('scalar', 'S', 1, ins)

    w.wait('vector', 'S', tok_xn)
    w.v('reduce_max', st[:, 8:9], b['xn'][:], axis=Ax.X, apply_absolute_value=True)
    w.drain('vector')
    w.v('tensor_scalar', st[:, 8:9], st[:, 8:9], 1e-12, None, Alu.max)
    w.drain('vector')
    w.v('tensor_scalar', st[:, 9:10], st[:, 8:9], C127_HI, None, Alu.mult)
    w.drain('vector')
    w.v('scalar_tensor_tensor', st[:, 10:11], st[:, 8:9], C127_LO, st[:, 9:10],
        Alu.mult, Alu.add)
    w.drain('vector')
    ins = w.v('reciprocal', st[:, 11:12], st[:, 10:11])
    tok_rls = w.inc('vector', 'V', 1, ins)
    w.wait('scalar', 'V', tok_rls)
    ins = w.s('activation', out=b['sA'][:], in_=b['xn'][:], func=AF.Identity,
              scale=st[:, 11:12])
    tok_y = w.inc('scalar', 'S', 1, ins)
    w.wait('vector', 'S', tok_y)
    w.wait('vector', 'TM', w.tok.get('qqT', 0))
    w.wait('vector', 'SP', w.tok.get('st_pl', 0))
    emit_round(w, b['sA'][:], b['sG'][:], b['sB'][:], b['sG'][:])
    ins = w.v('tensor_scalar', b['sG'][:], b['sG'][:], 127.0, -127.0, Alu.min, Alu.max)
    tok_qq = w.inc('vector', 'V', 1, ins)

    w.wait('tensor', 'V', tok_qq)
    for grp in range(2):
        def mms(bank, grp=grp):
            last = None
            for j in range(4):
                f = grp * 4 + j
                last = w.t('transpose', bank[:, j * 128:(j + 1) * 128],
                           b['sG'][:, f * 128:(f + 1) * 128], b['identF'][:])
            return last
        def evac(bank, grp=grp):
            w.wait('scalar', 'TM', w.tok.get('ffnM', 0))
            dst = _ap(b['qqf'], grp * 512, [[1024, 128], [1, 512]])
            return w.s('copy', dst, bank[:, 0:512])
        tokT = w.rotg(mms, 'scalar', evac)
    w.tok['qqT'] = tokT
    tok_qqf = w.c['S']

    for nb in range(2):
        def mms(bank, nb=nb):
            w.wait('tensor', 'S', tok_qqf)
            last = None
            for f in range(NF):
                last = w.t('matmul', bank[:, 0:512],
                           b['qqf'][:, f * 128:(f + 1) * 128],
                           b['qrhs'][:, f * 1024 + nb * 512: f * 1024 + (nb + 1) * 512],
                           start=(f == 0), stop=(f == NF - 1))
            return last
        def evac(bank, nb=nb):
            w.wait('vector', 'SP', w.tok.get('st_pl', 0))
            w.v('scalar_tensor_tensor', b['sA'][:, nb * 512:(nb + 1) * 512],
                bank[:, 0:512], st[:, 10:11], b['rsb'][:, nb * 512:(nb + 1) * 512],
                Alu.mult, Alu.mult)
            w.drain('vector')
            return w.v('tensor_tensor', b['sG'][:, nb * 512:(nb + 1) * 512],
                       b['sA'][:, nb * 512:(nb + 1) * 512],
                       X[:, nb * 512:(nb + 1) * 512], Alu.add)
        tokF = w.rotg(mms, 'vector', evac)
    w.tok['ffnM'] = tokF
    tok_pl = w.c['V']
    w.tok['pl_v%d' % (i % 2)] = tok_pl
    w.wait('gpsimd', 'V', tok_pl)
    w.tok['st_pl'] = w.dma('gpsimd', 'SP',
                           b['dram']['part'][s, i * 128:(i + 1) * 128, :], b['sG'][:])

    w.wait('tensor', 'S', tok_xn)
    for grp in range(2):
        def mms(bank, grp=grp):
            last = None
            for j in range(4):
                f = grp * 4 + j
                last = w.t('transpose', bank[:, j * 128:(j + 1) * 128],
                           b['xn'][:, f * 128:(f + 1) * 128], b['identF'][:])
            return last
        def evac(bank, grp=grp):
            w.wait('scalar', 'TM', w.tok.get('qkvM', 0))
            dst = _ap(b['xfm'], grp * 512, [[1024, 128], [1, 512]])
            return w.s('copy', dst, bank[:, 0:512])
        tokT = w.rotg(mms, 'scalar', evac)
    w.tok['xnT'] = tokT
    tok_xfm = w.c['S']

    dsts = [b['qt'], b['m1'], b['m2']]
    for pj in range(3):
        for nb in range(2):
            def mms(bank, pj=pj, nb=nb):
                w.wait('tensor', 'S', tok_xfm)
                last = None
                for f in range(NF):
                    rhs = b['arena'][:, f * 3072 + pj * 1024 + nb * 512:
                                     f * 3072 + pj * 1024 + (nb + 1) * 512]
                    last = w.t('matmul', bank[:, 0:512],
                               b['xfm'][:, f * 128:(f + 1) * 128], rhs,
                               start=(f == 0), stop=(f == NF - 1))
                return last
            def evac(bank, pj=pj, nb=nb):
                if pj == 0:
                    w.wait('scalar', 'V', w.tok.get('ropeq', 0))
                elif pj == 1:
                    w.wait('scalar', 'V', w.tok.get('ropek', 0))
                else:
                    w.wait('scalar', 'TM', w.tok.get('kvM', 0))
                return w.s('copy', dsts[pj][:, nb * 512:(nb + 1) * 512],
                           bank[:, 0:512])
            tokP = w.rotg(mms, 'scalar', evac)
    w.tok['qkvM'] = tokP
    tok_proj = w.c['S']

    ct = i
    def rope(src, dst, tok_src_ready, scale):
        w.wait('vector', 'S', tok_src_ready)
        c2 = _ap(b['ctab'], ct * 32, [[D, 128], [0, H], [0, 2], [1, 32]])
        s2 = _ap(b['stab'], ct * 64, [[2048, 128], [0, H], [32, 2], [1, 32]])
        srcv = _ap(src, 0, [[D, 128], [64, H], [32, 2], [1, 32]])
        srcw = _ap(src, 32, [[D, 128], [64, H], [-32, 2], [1, 32]])
        dstv = _ap(dst, 0, [[D, 128], [64, H], [32, 2], [1, 32]])
        sBv = _ap(b['sB'], 0, [[D, 128], [64, H], [32, 2], [1, 32]])
        w.v('tensor_tensor', dstv, srcv, c2, Alu.mult)
        w.v('tensor_tensor', sBv, srcw, s2, Alu.mult)
        w.drain('vector')
        w.v('tensor_tensor', dst[:], dst[:], b['sB'][:], Alu.add)
        w.drain('vector')
        w.v('tensor_scalar', b['sB'][:], dst[:], 0.0, None, Alu.min)
        w.drain('vector')
        ins = w.v('tensor_scalar', dst[:], dst[:], scale, 0.0, Alu.mult, Alu.max)
        tok_mx = w.inc('vector', 'V', 1, ins)
        w.wait('scalar', 'V', tok_mx)
        ins = w.s('activation', out=b['sB'][:], in_=b['sB'][:], func=AF.Exp,
                  scale=(b['cQS'][:] if scale != 1.0 else 1.0))
        tok_e = w.inc('scalar', 'S', 1, ins)
        w.wait('vector', 'S', tok_e)
        w.drain('vector')
        ins = w.v('tensor_tensor', dst[:], dst[:], b['sB'][:], Alu.add)
        return w.inc('vector', 'V', 1, ins)

    w.wait('vector', 'TM', w.tok.get('kvM', 0))
    tok_kf = rope(b['m1'], b['kr'], tok_proj, 1.0)
    w.tok['ropek'] = tok_kf
    w.wait('vector', 'SQ', w.tok.get('st_qf', 0))
    tok_qf = rope(b['qt'], b['qr'], tok_proj, ELU_SCALE)
    w.tok['ropeq'] = tok_qf

    w.wait('gpsimd', 'V', tok_qf)
    w.tok['st_qf'] = w.dma('gpsimd', 'SQ',
                           b['dram']['qfs'][s, i * 128:(i + 1) * 128, :], b['qr'][:])

    nt = w.ntiles
    w.wait('tensor', 'V', tok_kf)
    last = None
    for p in range(NPAIR):
        bank = b['pkv'][s * 2 + p // 4]
        first = (i == 0) and (p % 4 == 0)
        lastm = (i == nt - 1) and (p % 4 == 3)
        last = w.t('matmul', bank[:, (p % 4) * 128:(p % 4 + 1) * 128],
                   b['m2'][:, p * 128:(p + 1) * 128],
                   b['kr'][:, p * 128:(p + 1) * 128],
                   start=first, stop=lastm)
    tok_kv1 = w.inc('tensor', 'TM', 1, last)
    tokK = 0
    for half in range(2):
        def mms(bank, half=half):
            return w.t('matmul', bank[0:1, 0:512], b['onesF'][:, 0:1],
                       b['kr'][:, half * 512:(half + 1) * 512], start=True, stop=True)
        def evac(bank, half=half, s=s):
            dst = b['ksrow'][0:1, s * D + half * 512: s * D + (half + 1) * 512]
            w.drain('vector')
            return w.v('tensor_tensor', dst, dst, bank[0:1, 0:512], Alu.add)
        tokK = w.rotg(mms, 'vector', evac)
    w.tok['kvM'] = max(tok_kv1, tokK)


def midphase(w):
    b = w.b
    ar = b['arena']
    for oc in range(NF):
        stg = b['xa'] if oc % 2 == 0 else b['xb']
        w.wait('sync', 'V', w.c['V'])
        w.wait('sync', 'TM', w.c['TM'])
        sem = 'LDA' if oc % 2 == 0 else 'LDB'
        tok = w.dma('sync', sem, stg[:],
                    b['dram']['wout'][oc * 128:(oc + 1) * 128, :])
        w.wait('tensor', sem, tok)
        for grp in range(2):
            def mms(bank, stg=stg, grp=grp):
                last = None
                for j in range(4):
                    f = grp * 4 + j
                    last = w.t('transpose', bank[:, j * 128:(j + 1) * 128],
                               stg[:, f * 128:(f + 1) * 128], b['identF'][:])
                return last
            def evac(bank, oc=oc, grp=grp):
                dst = _ap(ar, (grp * 4) * 1024 + oc * 128,
                          [[24576, 128], [1024, 4], [1, 128]])
                src = _ap(bank, 0, [[512, 128], [128, 4], [1, 128]])
                return w.s('copy', dst, src)
            w.rotg(mms, 'scalar', evac)
    tok_wT = w.c['S']

    tok_m = 0
    for s in range(NSEQ):
        w.wait('vector', 'TM', max(w.tok['kvM'], tok_m))
        last = None
        for p in range(NPAIR):
            bank = b['pkv'][s * 2 + p // 4]
            base = p * 128
            last = w.v('tensor_copy', b['vksb'][0:64, base:base + 64],
                       bank[0:64, (p % 4) * 128:(p % 4) * 128 + 64])
            last = w.v('tensor_copy', b['vksb'][64:128, base + 64:base + 128],
                       bank[64:128, (p % 4) * 128 + 64:(p % 4) * 128 + 128])
        tok_vk = w.inc('vector', 'V', 1, last)
        for p in range(NPAIR):
            for nb in range(2):
                def mms(bank, s=s, p=p, nb=nb):
                    w.wait('tensor', 'S', tok_wT)
                    w.wait('tensor', 'V', tok_vk)
                    return w.t('matmul', bank[:, 0:512],
                               b['vksb'][:, p * 128:(p + 1) * 128],
                               ar[:, p * 1024 + nb * 512: p * 1024 + (nb + 1) * 512],
                               start=True, stop=True)
                def evac(bank, s=s, p=p, nb=nb):
                    dst = ar[:, 8192 + (s * NPAIR + p) * 1024 + nb * 512:
                             8192 + (s * NPAIR + p) * 1024 + (nb + 1) * 512]
                    return w.s('copy', dst, bank[:, 0:512])
                tok_m = w.rotg(mms, 'scalar', evac)


def pass2_seq(w, s):
    b = w.b
    nt = w.ntiles
    w.wait('sync', 'SQ', w.c['SQ'])
    w.wait('sync', 'SP', w.c['SP'])
    for half in range(2):
        def mms(bank, half=half):
            w.wait('tensor', 'V', w.tok['ksrow_done%d' % s])
            return w.t('matmul', bank[:, 0:512], b['onesR'][0:1, :],
                       b['ksrow'][0:1, s * D + half * 512: s * D + (half + 1) * 512],
                       start=True, stop=True)
        def evac(bank, half=half):
            w.drain('vector')
            return w.v('tensor_copy', b['ksb'][:, half * 512:(half + 1) * 512],
                       bank[:, 0:512])
        w.rotg(mms, 'vector', evac)
    for j in range(nt):
        QF = b['xa'] if j % 2 == 0 else b['xb']
        PT = b['m1'] if j % 2 == 0 else b['m2']
        if j == 0:
            w.wait('sync', 'V', w.c['V'])
            w.wait('sync', 'S', w.c['S'])
            w.wait('sync', 'TM', w.c['TM'])
            w.tok['ld_qf0'] = w.dma('sync', 'LDA', QF[:], b['dram']['qfs'][s, 0:128, :])
            w.tok['ld_pt0'] = w.dma('sync', 'LDC', PT[:], b['dram']['part'][s, 0:128, :])
            ldq, ldp = w.tok['ld_qf0'], w.tok['ld_pt0']
        else:
            ldq, ldp = w.tok['ld_qf%d' % (j % 2)], w.tok['ld_pt%d' % (j % 2)]
        if j + 1 < nt:
            QFn = b['xb'] if j % 2 == 0 else b['xa']
            PTn = b['m2'] if j % 2 == 0 else b['m1']
            w.wait('sync', 'V', w.tok.get('p2den%d' % ((j + 1) % 2), 0))
            w.wait('sync', 'V', w.tok.get('p2ev%d' % ((j + 1) % 2), 0))
            w.tok['ld_qf%d' % ((j + 1) % 2)] = w.dma(
                'sync', 'LDA' if (j + 1) % 2 == 0 else 'LDB',
                QFn[:], b['dram']['qfs'][s, (j + 1) * 128:(j + 2) * 128, :])
            w.tok['ld_pt%d' % ((j + 1) % 2)] = w.dma(
                'sync', 'LDC' if (j + 1) % 2 == 0 else 'LDD',
                PTn[:], b['dram']['part'][s, (j + 1) * 128:(j + 2) * 128, :])
        tile_p2(w, s, j, QF, PT, ldq, ldp)


def tile_p2(w, s, j, QF, PT, ldq, ldp):
    b = w.b
    st = b['stat']
    qsem = 'LDA' if j % 2 == 0 else 'LDB'
    psem = 'LDC' if j % 2 == 0 else 'LDD'
    w.wait('vector', qsem, ldq)
    w.v('tensor_tensor', b['sA'][:], QF[:], b['ksb'][:], Alu.mult)
    w.drain('vector')
    dred = _ap(st, 16, [[64, 128], [1, H], [0, 1]])
    din = _ap(b['sA'], 0, [[D, 128], [64, H], [1, 64]])
    w.v('reduce_sum', dred, din, axis=Ax.X)
    w.drain('vector')
    w.v('tensor_scalar', st[:, 16:32], st[:, 16:32], 1e-6, None, Alu.max)
    w.drain('vector')
    w.v('reciprocal', st[:, 32:48], st[:, 16:32])
    w.drain('vector')
    w.wait('vector', 'TM', w.tok.get('qsT', 0))
    ins = None
    for h in range(H):
        ins = w.v('tensor_scalar', b['xn'][:, h * 64:(h + 1) * 64],
                  QF[:, h * 64:(h + 1) * 64], st[:, 32 + h:33 + h], None, Alu.mult)
    tok_qs = w.inc('vector', 'V', 1, ins)
    w.tok['p2den%d' % (j % 2)] = tok_qs

    w.wait('tensor', 'V', tok_qs)
    for grp in range(2):
        def mms(bank, grp=grp):
            last = None
            for jj in range(4):
                f = grp * 4 + jj
                last = w.t('transpose', bank[:, jj * 128:(jj + 1) * 128],
                           b['xn'][:, f * 128:(f + 1) * 128], b['identF'][:])
            return last
        def evac(bank, grp=grp):
            w.wait('scalar', 'TM', w.tok.get('mainM', 0))
            dst = _ap(b['xfm'], grp * 512, [[1024, 128], [1, 512]])
            return w.s('copy', dst, bank[:, 0:512])
        tokT = w.rotg(mms, 'scalar', evac)
    w.tok['qsT'] = tokT
    tok_qfm = w.c['S']

    for nb in range(2):
        def mms(bank, nb=nb):
            w.wait('tensor', 'S', tok_qfm)
            last = None
            for p in range(NPAIR):
                rhs = b['arena'][:, 8192 + (s * NPAIR + p) * 1024 + nb * 512:
                                 8192 + (s * NPAIR + p) * 1024 + (nb + 1) * 512]
                last = w.t('matmul', bank[:, 0:512],
                           b['xfm'][:, p * 128:(p + 1) * 128], rhs,
                           start=(p == 0), stop=(p == NPAIR - 1))
            return last
        def evac(bank, nb=nb):
            w.wait('vector', psem, ldp)
            w.wait('vector', 'SY', w.tok.get('st_y', 0))
            w.drain('vector')
            return w.v('scalar_tensor_tensor', b['qt'][:, nb * 512:(nb + 1) * 512],
                       bank[:, 0:512], 1.0, PT[:, nb * 512:(nb + 1) * 512],
                       Alu.mult, Alu.add)
        tokM = w.rotg(mms, 'vector', evac)
    w.tok['mainM'] = tokM
    tok_out = w.c['V']
    w.tok['p2ev%d' % (j % 2)] = tok_out

    w.wait('gpsimd', 'V', tok_out)
    w.tok['st_y'] = w.dma('gpsimd', 'SY',
                          b['dram']['y'][s, j * 128:(j + 1) * 128, :], b['qt'][:])


def kernel(L0_attn_x, L0_attn_w_qkv, L0_attn_w_out, L0_ffn_rhs):
    x = np.ascontiguousarray(np.asarray(L0_attn_x, dtype=np.float32))
    wqkv = np.ascontiguousarray(np.asarray(L0_attn_w_qkv, dtype=np.float32))
    wout = np.ascontiguousarray(np.asarray(L0_attn_w_out, dtype=np.float32))
    frhs = np.ascontiguousarray(np.asarray(L0_ffn_rhs, dtype=np.float32))
    B, T, _ = x.shape
    n_cores = 8
    per = B // n_cores
    nc = build_nc(seq_t=T)
    in_maps = []
    for c in range(n_cores):
        in_maps.append({
            "x": x[c * per:(c + 1) * per],
            "wqkv": wqkv,
            "wout": wout,
            "frhs": frhs,
        })
    res = run_bass_kernel_spmd(nc, in_maps, core_ids=list(range(n_cores)))
    return np.concatenate([r["y"] for r in res.results], axis=0)
